# version 17
# speedup vs baseline: 1.0821x; 1.0163x over previous
import numpy as np
import ml_dtypes
import concourse.bass as bass
import concourse.mybir as mybir
from concourse.bass_utils import run_bass_kernel_spmd
from concourse.bass_types import DynSlice
from concourse.bass import IndirectOffsetOnAxis
from concourse.expressions import make_scalar_value
from contextlib import ExitStack

F32 = mybir.dt.float32
BF16 = mybir.dt.bfloat16
U32 = mybir.dt.uint32
U16 = mybir.dt.uint16
AF = mybir.ActivationFunctionType
OP = mybir.AluOpType
AX = mybir.AxisListType

B, N, D, E, DH = 4, 8192, 1024, 8, 4096
TOK, CH = 4096, 512
NCH = N // CH
MYT = TOK // 128
CAP = 1152
XROWS = TOK + 128
NORM = 1.0 / (64.0 ** 0.25)
EPS_AT = 1e-6
EPS_LN = 1e-5
NBIS = 26


class Sem:
    def __init__(self, ctx, nc, name):
        self.h = ctx.enter_context(nc.semaphore(name))
        self.n = 0

    def inc(self, inst, k=1):
        inst.then_inc(self.h, k)
        self.n += k
        return self.n

    def raw(self, inst, k=1):
        inst.then_inc(self.h, k)
        return inst


def build_program(debug_level=0, stop_stage=None):
    nc = bass.Bass("TRN2", target_bir_lowering=False)

    def din(name, shape, dt=F32):
        return nc.dram_tensor(name, shape, dt, kind="ExternalInput").ap()

    xT_d = din("xT", [D, N])
    wq_d = din("wq_t4", [8, 8, 128, 128])
    wk_d = din("wk_t4", [8, 8, 128, 128])
    wo_d = din("wo_t4", [8, 8, 128, 128])
    wvT_d = din("wvT", [D, D])
    bq_d = din("bq_c", [128, 8])
    bk_d = din("bk_c", [128, 8])
    bo_d = din("bo_c", [128, 8])
    bv_d = din("bv_r", [1, D])
    pT_d = din("pT", [64, 64])
    gwT_d = din("gwT", [D, 8])
    w1_d = din("w1r", [E, 32, D, 128])
    w2_d = din("w2r", [E, DH, D])
    idf_d = din("ident_f", [128, 128])
    idb_d = din("ident_b", [128, 128], BF16)
    tri_d = din("tri128_b", [128, 128], BF16)
    tis_d = din("triS32_b", [32, 32], BF16)
    iosq_d = din("iosq_f", [128, 128])
    iocf_d = din("iocol_f", [128, 1])
    iocb_d = din("iocol_b", [128, 1], BF16)
    trsh_d = din("trash_f", [128, 1])
    big8_d = din("big8_f", [128, 8])
    o128b_d = din("ones128_1_b", [128, 1], BF16)
    o1x128f_d = din("ones1_128_f", [1, 128])
    i128_8_d = din("ind128_8_f", [128, 8])
    i8_128_d = din("ind8_128_f", [8, 128])
    ind8p_d = din("ind8p_f", [16, 128])
    o32sq_d = din("ones32_32_b", [32, 32], BF16)
    iB128_16_d = din("indB128_16_f", [128, 16])
    iB16_128_d = din("indB16_128_f", [16, 128])
    iPair_d = din("indPair_f", [16, 8 * 128], BF16)

    out_d = nc.dram_tensor("out", [TOK, D], F32, kind="ExternalOutput").ap()
    dbg = {}
    if debug_level >= 1:
        dbg["x1"] = nc.dram_tensor("dbg_x1", [TOK, D], F32, kind="ExternalOutput").ap()
        dbg["scores"] = nc.dram_tensor("dbg_scores", [8, N], F32, kind="ExternalOutput").ap()
    if debug_level >= 2:
        dbg["theta"] = nc.dram_tensor("dbg_theta", [8, 1], F32, kind="ExternalOutput").ap()
        dbg["counts"] = nc.dram_tensor("dbg_counts", [1, 8], F32, kind="ExternalOutput").ap()
        dbg["offs"] = nc.dram_tensor("dbg_offs", [32, 8], F32, kind="ExternalOutput").ap()
        dbg["xin0"] = nc.dram_tensor("dbg_xin0", [XROWS, 1040], U16, kind="ExternalOutput").ap()

    scores_d = nc.dram_tensor("scores_s", [8, N], F32, kind="Internal").ap()
    offs_d = nc.dram_tensor("offs_s", [32, 8], F32, kind="Internal").ap()
    acc_d = nc.dram_tensor("acc_s", [TOK + 128, D], F32, kind="Internal").ap()
    abscr_d = nc.dram_tensor("abscr_s", [2, 512], F32, kind="Internal").ap()
    w1b_d = nc.dram_tensor("w1b_s", [E, 32, 1024, 128], BF16, kind="Internal").ap()
    w2b_d = nc.dram_tensor("w2b_s", [E, 32, 128, 1024], BF16, kind="Internal").ap()
    xin_ds = [nc.dram_tensor(f"xin_s{e}", [XROWS, 1040], BF16, kind="Internal").ap()
              for e in range(E)]

    ctx = ExitStack()
    with ctx:
        s_ld = Sem(ctx, nc, "s_ld")
        s_gd = Sem(ctx, nc, "s_gd")
        s_sc = Sem(ctx, nc, "s_sc")
        s_t = Sem(ctx, nc, "s_t")
        s_v = Sem(ctx, nc, "s_v")
        s_s = Sem(ctx, nc, "s_s")
        s_wr = [Sem(ctx, nc, f"s_wr{e}") for e in range(E)]
        s_cat = Sem(ctx, nc, "s_cat")
        s_out = Sem(ctx, nc, "s_out")

        block = ctx.enter_context(nc.Block())
        T, V, S, SY, G = (block.tensor, block.vector, block.scalar,
                          block.sync, block.gpsimd)

        def barrier():
            sems = tuple((s.h, s.n) for s in
                         [s_ld, s_gd, s_sc, s_t, s_v, s_s, s_cat] + s_wr if s.n)
            for engf in (T, V, S, SY, G):
                def _f(eng, _sems=sems):
                    for h, n in _sems:
                        eng.wait_ge(h, n)
                engf(_f)

        def sbuf(c, name, shape, dt=F32):
            return c.enter_context(nc.sbuf_tensor(name, shape, dt))

        def psum(c, name):
            return c.enter_context(nc.psum_tensor(name, [128, 512], F32))

        idf = sbuf(ctx, "idf", [128, 128])
        idb = sbuf(ctx, "idb", [128, 128], BF16)
        iocf = sbuf(ctx, "iocf", [128, 1])
        trshf = sbuf(ctx, "trshf", [128, 1])
        o128b = sbuf(ctx, "o128b", [128, 1], BF16)
        o1x128f = sbuf(ctx, "o1x128f", [1, 128])
        i128_8 = sbuf(ctx, "i128_8", [128, 8])
        i8_128 = sbuf(ctx, "i8_128", [8, 128])
        ind8p = sbuf(ctx, "ind8p", [16, 128])
        o32sq = sbuf(ctx, "o32sq", [32, 32], BF16)
        iB128_16 = sbuf(ctx, "iB128_16", [128, 16])
        iB16_128 = sbuf(ctx, "iB16_128", [16, 128])
        iPair = sbuf(ctx, "iPair", [16, 8 * 128], BF16)
        gwT = sbuf(ctx, "gwT_s", [128, 64])
        bqn = sbuf(ctx, "bqn", [128, 8])
        bkn = sbuf(ctx, "bkn", [128, 8])
        boc = sbuf(ctx, "boc", [128, 8])
        blkP = sbuf(ctx, "blkP", [128, 128], BF16)
        blkKV = sbuf(ctx, "blkKV", [128, 8 * 128], BF16)
        ksB = sbuf(ctx, "ksB", [128, 8 * 16], BF16)
        thB8 = sbuf(ctx, "thB8", [128, 8])
        countB = sbuf(ctx, "countB", [128, 8])
        offbu = sbuf(ctx, "offbu", [1, 256], U32)
        offsc = sbuf(ctx, "offsc", [1, 256])
        counts1 = sbuf(ctx, "counts1", [1, 8])

        def _ld(sync):
            for t, d_ in ((idf[:], idf_d), (iocf[:], iocf_d),
                          (trshf[:], trsh_d), (o1x128f[:], o1x128f_d),
                          (i128_8[:], i128_8_d), (i8_128[:], i8_128_d),
                          (ind8p[:], ind8p_d), (o32sq[:], o32sq_d),
                          (iB128_16[:], iB128_16_d), (iB16_128[:], iB16_128_d),
                          (iPair[:], iPair_d), (idb[:], idb_d),
                          (iocb_pl[0][:], iocb_d), (o128b[:], o128b_d),
                          (gwT[:].rearrange("p (dc e) -> p dc e", dc=8),
                           gwT_d.rearrange("(dc p) e -> p dc e", p=128)),
                          (bqn[:], bq_d), (bkn[:], bk_d), (boc[:], bo_d),
                          (big8_pl[0][:], big8_d),
                          (tri_pl[0][:], tri_d), (tis_pl[0][:], tis_d),
                          (iosq_pl[0][:], iosq_d)):
                s_ld.inc(sync.dma_start(out=t, in_=d_), 16)

        mid = ctx.enter_context(ExitStack())
        iocb_pl = [sbuf(mid, "iocb", [128, 1], BF16)]
        big8_pl = [sbuf(mid, "big8", [128, 8])]
        tri_pl = [sbuf(mid, "tri", [128, 128], BF16)]
        tis_pl = [sbuf(mid, "tis", [32, 32], BF16)]
        iosq_pl = [sbuf(mid, "iosq", [128, 128])]
        iocb, big8, tri, tis, iosq = (iocb_pl[0], big8_pl[0], tri_pl[0],
                                      tis_pl[0], iosq_pl[0])
        ptokf = sbuf(mid, "ptokf", [128, MYT * 8])
        ptokb = sbuf(mid, "ptokb", [128, MYT * 8], BF16)
        slotv = sbuf(mid, "slotv", [128, MYT * 8])
        x1tokb = sbuf(mid, "x1tokb", [128, MYT * 1024], BF16)

        SY(_ld)
        lds0 = s_ld.n

        def _prep(v):
            v.wait_ge(s_ld.h, lds0)
            s_v.inc(v.tensor_scalar(bqn[:], bqn[:], NORM, None, OP.mult))
            s_v.inc(v.tensor_scalar(bkn[:], bkn[:], NORM, None, OP.mult))
            s_v.inc(v.memset(blkP[:], 0.0))
            s_v.inc(v.memset(blkKV[:], 0.0))
            s_v.inc(v.memset(ksB[:], 0.0))
        V(_prep)
        prep_ck = s_v.n

        def _gblk(g):
            g.wait_ge(s_v.h, prep_ck)
            s_gd.inc(g.dma_start(out=blkP[0:64, 0:64], in_=pT_d), 16)
            s_gd.inc(g.dma_start(out=blkP[64:128, 64:128], in_=pT_d), 16)
        G(_gblk)

        attn = ctx.enter_context(ExitStack())
        wqT = sbuf(attn, "wqT", [128, 8 * 1024], BF16)
        woT = sbuf(attn, "woT", [128, 8 * 1024], BF16)
        xb = [sbuf(attn, f"xb{i}", [128, 8 * CH], BF16) for i in range(2)]
        attnA = ctx.enter_context(ExitStack())
        wkT = sbuf(attnA, "wkT", [128, 8 * 1024], BF16)
        wvM = sbuf(attnA, "wvM", [128, 8 * 1024], BF16)
        bvB = sbuf(attnA, "bvB", [128, D])

        def _gw(g):
            s_gd.inc(g.dma_start(
                out=wkT[:].rearrange("p (a b c) -> p a b c", a=8, b=8),
                in_=wk_d.rearrange("kc dc p q -> p kc dc q")), 16)
            s_gd.inc(g.dma_start(
                out=wvM[:].rearrange("p (a b) -> p a b", a=8),
                in_=wvT_d.rearrange("(dc p) q -> p dc q", p=128)), 16)
            s_gd.inc(g.dma_start(
                out=wqT[:].rearrange("p (a b c) -> p a b c", a=8, b=8),
                in_=wq_d.rearrange("kc dc p q -> p kc dc q")), 16)
            s_gd.inc(g.dma_start(
                out=woT[:].rearrange("p (a b c) -> p a b c", a=8, b=8),
                in_=wo_d.rearrange("kc dc p q -> p kc dc q")), 16)
        G(_gw)
        wdma_ck = s_gd.n

        with nc.sbuf_tensor("bv1", [1, D], F32) as bv1, \
             nc.psum_tensor("p_bv", [128, 512], F32) as p_bv:
            def _ldbv(sync):
                s_ld.inc(sync.dma_start(out=bv1[:], in_=bv_d), 16)
            SY(_ldbv)
            ldbv = s_ld.n
            for half in range(2):
                def _t(t, half=half, vck=s_v.n, ldbv=ldbv):
                    t.wait_ge(s_ld.h, max(lds0, ldbv))
                    t.wait_ge(s_v.h, vck)
                    s_t.inc(t.matmul(p_bv[:], o1x128f[:],
                                     bv1[0:1, half * 512:(half + 1) * 512],
                                     start=True, stop=True))
                T(_t)

                def _v(v, half=half, tck=s_t.n):
                    v.wait_ge(s_t.h, tck)
                    s_v.inc(v.tensor_copy(bvB[:, half * 512:(half + 1) * 512],
                                          p_bv[:]))
                V(_v)

        barrier()

        xb_ck = [0, 0]
        xb_free = [0, 0]

        def emit_conv(i0):
            def _g(g, i0=i0):
                for i in range(i0, min(i0 + 16, 512)):
                    if i < 256:
                        e_, f_ = divmod(i, 32)
                        s_gd.inc(g.dma_start(out=w1b_d[e_, f_],
                                             in_=w1_d[e_, f_]), 16)
                    else:
                        e_, fc_ = divmod(i - 256, 32)
                        s_gd.inc(g.dma_start(
                            out=w2b_d[e_, fc_],
                            in_=w2_d[e_].rearrange("(fc p) q -> fc p q",
                                                   p=128)[fc_]), 16)
            G(_g)

        def load_x_chunk(c):
            buf = c % 2
            def _g(g, c=c, buf=buf, free=xb_free[buf]):
                if free:
                    g.wait_ge(s_t.h, free)
                s_gd.inc(g.dma_start(
                    out=xb[buf][:].rearrange("p (dc n) -> p dc n", dc=8),
                    in_=xT_d.rearrange("(dc p) n -> p dc n", p=128)[:, :, c * CH:(c + 1) * CH]
                ), 16)
            G(_g)
            xb_ck[buf] = s_gd.n
            return buf

        with ExitStack() as pa:
            kb = [sbuf(pa, f"kb{i}", [128, CH], BF16) for i in range(2)]
            kftok = sbuf(pa, "kftok", [128, 4 * 1024], BF16)
            vtok = sbuf(pa, "vtok", [128, 4 * 1024], BF16)
            pK = [psum(pa, f"pa_k{i}") for i in range(2)]
            pF = [psum(pa, f"pa_f{i}") for i in range(2)]
            p_kv = [psum(pa, f"p_kv{i}") for i in range(3)]
            KVOF = 132
            evK = [0, 0]
            kfmm = [0, 0]
            evF = [0, 0]
            vev = [0, 0]

            for c in range(NCH):
                buf = load_x_chunk(c)
                emit_conv(c * 16)

                def emit_K(kc, buf=None):
                    pb_ = kc % 2
                    def _t(t, buf=buf, kc=kc, pb_=pb_, ldk=xb_ck[buf],
                           ek=evK[pb_], vv=vev[pb_]):
                        t.wait_ge(s_gd.h, max(ldk, wdma_ck))
                        if ek:
                            t.wait_ge(s_s.h, ek)
                        if vv:
                            t.wait_ge(s_v.h, vv)
                        for dc in range(8):
                            s_t.inc(t.matmul(
                                pK[pb_][:],
                                wkT[:, kc * 1024 + dc * 128:kc * 1024 + (dc + 1) * 128],
                                xb[buf][:, dc * CH:(dc + 1) * CH],
                                start=(dc == 0), stop=(dc == 7)))
                    T(_t)
                    tK = s_t.n

                    def _sfn(s, kc=kc, pb_=pb_, tK=tK, kfc=kfmm[pb_]):
                        s.wait_ge(s_t.h, tK)
                        if kfc:
                            s.wait_ge(s_t.h, kfc)
                        s_s.raw(s.activation(kb[pb_][:], pK[pb_][:], AF.Identity,
                                             bias=bkn[:, kc:kc + 1], scale=NORM))
                    kev_defer.append(_sfn)
                    s_s.n += 1
                    evK[pb_] = s_s.n

                def emit_Kf(kc):
                    pb_ = kc % 2
                    def _t2(t, kc=kc, pb_=pb_, ek=evK[pb_], ef=evF[pb_],
                            vv=vev[pb_]):
                        t.wait_ge(s_s.h, ek)
                        if max(ef, vv):
                            t.wait_ge(s_v.h, max(ef, vv))
                        for ts in range(4):
                            s_t.inc(t.matmul(
                                pF[pb_][:, ts * 128:(ts + 1) * 128],
                                kb[pb_][:, ts * 128:(ts + 1) * 128],
                                blkP[:], start=(ts == 0), stop=(ts == 3)))
                    T(_t2)
                    kfmm[pb_] = s_t.n
                    tF = s_t.n

                    def _v(v, kc=kc, pb_=pb_, tF=tF):
                        v.wait_ge(s_t.h, tF)
                        s_v.inc(v.tensor_scalar(
                            kftok[:].rearrange("p (ts r) -> p ts r", ts=4)
                                [:, :, kc * 128:(kc + 1) * 128],
                            pF[pb_][:].rearrange("p (ts r) -> p ts r", ts=4),
                            0.0, EPS_AT, OP.max, OP.add))
                    V(_v)
                    evF[pb_] = s_v.n

                kev_defer = []
                for kc in range(8):
                    emit_K(kc, buf=buf)
                    if kc >= 1:
                        emit_Kf(kc - 1)
                emit_Kf(7)

                def _skall(s, fns=tuple(kev_defer)):
                    for fn in fns:
                        fn(s)
                S(_skall)

                for ts in range(4):
                    pv0, pv1 = (pK[0], pK[1]) if ts % 2 == 0 else (pF[0], pF[1])
                    def _t(t, buf=buf, ts=ts, pv0=pv0, pv1=pv1,
                           vv=vev[ts % 2], ek=evK[:], ef=evF[:]):
                        if ts < 2:
                            t.wait_ge(s_s.h, max(ek))
                            t.wait_ge(s_v.h, max(ef))
                        if vv:
                            t.wait_ge(s_v.h, vv)
                        for dc in range(8):
                            s_t.inc(t.matmul(
                                pv0[:],
                                xb[buf][:, dc * CH + ts * 128:dc * CH + (ts + 1) * 128],
                                wvM[:, dc * 1024:dc * 1024 + 512],
                                start=(dc == 0), stop=(dc == 7)))
                            s_t.inc(t.matmul(
                                pv1[:],
                                xb[buf][:, dc * CH + ts * 128:dc * CH + (ts + 1) * 128],
                                wvM[:, dc * 1024 + 512:(dc + 1) * 1024],
                                start=(dc == 0), stop=(dc == 7)))
                    T(_t)
                    tV = s_t.n

                    def _v(v, ts=ts, pv0=pv0, pv1=pv1, tV=tV):
                        v.wait_ge(s_t.h, tV)
                        s_v.inc(v.tensor_tensor(vtok[:, ts * 1024:ts * 1024 + 512],
                                                pv0[:], bvB[:, 0:512], OP.add))
                        s_v.inc(v.tensor_tensor(vtok[:, ts * 1024 + 512:(ts + 1) * 1024],
                                                pv1[:], bvB[:, 512:1024], OP.add))
                    V(_v)
                    vev[ts % 2] = s_v.n
                xb_free[buf] = s_t.n
                v_done = s_v.n

                def _t(t, c=c, v_done=v_done):
                    t.wait_ge(s_v.h, v_done)
                    for ts in range(4):
                        for pr in range(8):
                            bank, off = divmod(pr, 3)
                            first = (c == 0 and ts == 0 and off == 0)
                            last = (c == NCH - 1 and ts == 3)
                            s_t.inc(t.matmul(
                                p_kv[bank][:, off * KVOF:off * KVOF + 128],
                                kftok[:, ts * 1024 + pr * 128:ts * 1024 + (pr + 1) * 128],
                                vtok[:, ts * 1024 + pr * 128:ts * 1024 + (pr + 1) * 128],
                                start=first, stop=last))
                            s_t.inc(t.matmul(
                                p_kv[bank][:, off * KVOF + 128:off * KVOF + 129],
                                kftok[:, ts * 1024 + pr * 128:ts * 1024 + (pr + 1) * 128],
                                o128b[:], start=False, stop=last))
                T(_t)
            kv_done = s_t.n

            def _v(v, kv_done=kv_done):
                v.wait_ge(s_t.h, kv_done)
                for pr in range(8):
                    bank, off = divmod(pr, 3)
                    src_ = p_kv[bank]
                    s_v.inc(v.tensor_copy(blkKV[0:64, pr * 128:pr * 128 + 64],
                                          src_[0:64, off * KVOF:off * KVOF + 64]))
                    s_v.inc(v.tensor_copy(blkKV[64:128, pr * 128 + 64:(pr + 1) * 128],
                                          src_[64:128, off * KVOF + 64:off * KVOF + 128]))
                    s_v.inc(v.tensor_copy(ksB[0:64, pr * 16 + 2 * pr:pr * 16 + 2 * pr + 1],
                                          src_[0:64, off * KVOF + 128:off * KVOF + 129]))
                    s_v.inc(v.tensor_copy(ksB[64:128, pr * 16 + 2 * pr + 1:pr * 16 + 2 * pr + 2],
                                          src_[64:128, off * KVOF + 128:off * KVOF + 129]))
            V(_v)
            barrier()

        attnA.close()

        with ExitStack() as pb:
            qb = [sbuf(pb, f"qb{i}", [128, CH], BF16) for i in range(2)]
            qfb = sbuf(pb, "qfb", [128, 8 * CH], BF16)
            att = sbuf(pb, "att", [128, 8 * CH], BF16)
            rbuf = [sbuf(pb, f"rbuf{i}", [128, 8 * CH], F32) for i in range(2)]
            rb16 = sbuf(pb, "rb16", [128, 8 * CH], BF16)
            rb2 = att
            rec_sb = sbuf(pb, "rec_sb", [16, CH])
            rec16 = sbuf(pb, "rec16", [16, CH], BF16)
            onecol = sbuf(pb, "onecol", [128, 1])
            gwsr = sbuf(pb, "gwsr", [1, 64])
            gws8 = sbuf(pb, "gws8", [1, 8])
            gws_col = sbuf(pb, "gws_col", [8, 1])
            abTa = sbuf(pb, "abTa", [128, 4])
            abTb = sbuf(pb, "abTb", [128, 4])
            aB8sb = sbuf(pb, "aB8sb", [8, 512])
            rcb_sb = sbuf(pb, "rcb_sb", [128, CH])
            stat = sbuf(pb, "stat", [1, 5 * CH])
            sc_sb = sbuf(pb, "sc_sb", [8, CH])
            x1tf = [sbuf(pb, "x1tf0", [128, 1024])]
            x1tf.append(x1tf[0])
            pq = [psum(pb, f"pb_q{i}") for i in range(2)]
            pf = [psum(pb, f"pb_f{i}") for i in range(2)]
            p_den = psum(pb, "p_den")
            p_st = psum(pb, "p_st")
            p_tp = [psum(pb, f"pb_tp{i}") for i in range(2)]

            evQ = [0, 0]
            qfmm = [0, 0]
            evFq = [0, 0]
            att_ev = {}
            r_ev = {}
            x1tf_ck = [0, 0]
            tp_evv = [0, 0]
            tp_evs = [0, 0]
            tokb_ck = [0, 0]
            st_free = 0
            den_free = 0

            m_ap = stat[0:1, 0:CH]
            sd_ap = stat[0:1, CH:2 * CH]
            ab_ap = stat[0:1, 2 * CH:3 * CH]
            msq_ap = stat[0:1, 3 * CH:4 * CH]
            m2_ap = stat[0:1, 4 * CH:5 * CH]
            ve_ap = rcb_sb[0:1, 0:CH]

            state = {}

            def stage1a(c):
                buf = c % 2

                def emit_Q(kc):
                    pb_ = kc % 2
                    def _t(t, buf=buf, kc=kc, pb_=pb_, ldk=xb_ck[buf],
                           eq=evQ[pb_], ae=att_ev.get(pb_, 0),
                           rv_=r_ev.get(pb_, 0)):
                        t.wait_ge(s_gd.h, ldk)
                        if rv_:
                            t.wait_ge(s_s.h, rv_)
                        if max(eq, ae):
                            t.wait_ge(s_v.h, max(eq, ae))
                        for dc in range(8):
                            s_t.inc(t.matmul(
                                pq[pb_][:],
                                wqT[:, kc * 1024 + dc * 128:kc * 1024 + (dc + 1) * 128],
                                xb[buf][:, dc * CH:(dc + 1) * CH],
                                start=(dc == 0), stop=(dc == 7)))
                    T(_t)
                    tQ = s_t.n

                    def _v(v, kc=kc, pb_=pb_, tQ=tQ, qfc=qfmm[pb_]):
                        v.wait_ge(s_t.h, tQ)
                        if qfc:
                            v.wait_ge(s_t.h, qfc)
                        s_v.inc(v.tensor_scalar(qb[pb_][:], pq[pb_][:],
                                                NORM, bqn[:, kc:kc + 1],
                                                OP.mult, OP.add))
                    V(_v)
                    evQ[pb_] = s_v.n

                def emit_Qf(kc):
                    pb_ = kc % 2
                    def _t2(t, pb_=pb_, eq=evQ[pb_], ef=evFq[pb_],
                            ae=att_ev.get(2 + pb_, 0),
                            rv_=r_ev.get(2 + pb_, 0)):
                        if rv_:
                            t.wait_ge(s_s.h, rv_)
                        t.wait_ge(s_v.h, max(eq, ef, ae))
                        s_t.inc(t.matmul(pf[pb_][:], blkP[:], qb[pb_][:],
                                         start=True, stop=True))
                    T(_t2)
                    qfmm[pb_] = s_t.n
                    tF = s_t.n

                    def _v(v, kc=kc, pb_=pb_, tF=tF):
                        v.wait_ge(s_t.h, tF)
                        s_v.inc(v.tensor_scalar(qfb[:, kc * CH:(kc + 1) * CH],
                                                pf[pb_][:], 0.0, EPS_AT,
                                                OP.max, OP.add))
                    V(_v)
                    evFq[pb_] = s_v.n

                for kc in range(8):
                    emit_Q(kc)
                    if kc >= 1:
                        emit_Qf(kc - 1)
                emit_Qf(7)
                state["qf_done", c] = s_v.n
                state["xbfree", c] = s_t.n
                xb_free[buf] = s_t.n

            def stage1b_den(c):
                def _t(t, qd=state["qf_done", c], df=den_free,
                       sv=state.get(("a8cp", c - 1), 0)):
                    t.wait_ge(s_v.h, max(qd, df, sv))
                    for c8 in range(8):
                        s_t.inc(t.matmul(p_den[0:16, :],
                                         ksB[:, c8 * 16:(c8 + 1) * 16],
                                         qfb[:, c8 * CH:(c8 + 1) * CH],
                                         start=(c8 == 0), stop=(c8 == 7)))
                T(_t)
                tD = s_t.n

                def _v(v, tD=tD):
                    v.wait_ge(s_t.h, tD)
                    s_v.inc(v.tensor_scalar(rec_sb[:], p_den[0:16, :], EPS_AT,
                                            None, OP.add))
                    s_v.inc(v.reciprocal(rec_sb[:], rec_sb[:]))
                    v.wait_ge(s_v.h, s_v.n)
                    s_v.inc(v.tensor_copy(rec16[:], rec_sb[:]))
                V(_v)
                rec_ck = s_v.n

                state["recck", c] = rec_ck

            def stage1b_attn(c):
                rb = rbuf[c % 2]
                rec_ck = state["recck", c]
                rot = [pq[0], pq[1], pf[0], pf[1]]
                bc_ev = {}
                for c8 in range(8):
                    rb_ = c8 % 4
                    def _t(t, c8=c8, rb_=rb_, rec_ck=rec_ck,
                           ae=att_ev.get(rb_, 0), rv_=r_ev.get(rb_, 0),
                           be=bc_ev.get(rb_, 0)):
                        t.wait_ge(s_v.h, max(rec_ck, ae, be))
                        if rv_:
                            t.wait_ge(s_s.h, rv_)
                        s_t.inc(t.matmul(rot[rb_][:],
                                         iPair[:, c8 * 128:(c8 + 1) * 128],
                                         rec16[:], start=True, stop=True))
                    T(_t)
                    tB = s_t.n

                    def _v(v, c8=c8, rb_=rb_, tB=tB):
                        v.wait_ge(s_t.h, tB)
                        s_v.inc(v.tensor_copy(att[:, c8 * CH:(c8 + 1) * CH],
                                              rot[rb_][:]))
                    V(_v)
                    bc_ev[rb_] = s_v.n
                for c8 in range(8):
                    rb_ = c8 % 4
                    def _t(t, c8=c8, rb_=rb_, be=bc_ev.get(rb_, 0),
                           ae=att_ev.get(rb_, 0)):
                        t.wait_ge(s_v.h, max(be, ae))
                        s_t.inc(t.matmul(rot[rb_][:],
                                         blkKV[:, c8 * 128:(c8 + 1) * 128],
                                         qfb[:, c8 * CH:(c8 + 1) * CH],
                                         start=True, stop=True))
                    T(_t)
                    tA = s_t.n

                    def _v(v, c8=c8, rb_=rb_, tA=tA):
                        v.wait_ge(s_t.h, tA)
                        s_v.inc(v.tensor_tensor(att[:, c8 * CH:(c8 + 1) * CH],
                                                rot[rb_][:],
                                                att[:, c8 * CH:(c8 + 1) * CH],
                                                OP.mult))
                    V(_v)
                    att_ev[rb_] = s_v.n
                att_done = s_v.n

                wo_defer = []
                for oc in range(8):
                    ro = oc % 4
                    def _t(t, oc=oc, ro=ro, att_done=att_done,
                           ae=att_ev.get(ro, 0), re=r_ev.get(ro, 0)):
                        t.wait_ge(s_v.h, max(att_done, ae))
                        if re:
                            t.wait_ge(s_s.h, re)
                        for dc in range(8):
                            s_t.inc(t.matmul(
                                rot[ro][:],
                                woT[:, oc * 1024 + dc * 128:oc * 1024 + (dc + 1) * 128],
                                att[:, dc * CH:(dc + 1) * CH],
                                start=(dc == 0), stop=(dc == 7)))
                    T(_t)
                    tW = s_t.n

                    def _sfn(s, oc=oc, ro=ro, tW=tW, rb=rb):
                        s.wait_ge(s_t.h, tW)
                        s_s.raw(s.activation(rb[:, oc * CH:(oc + 1) * CH],
                                             rot[ro][:], AF.Identity,
                                             bias=boc[:, oc:oc + 1]))
                    wo_defer.append(_sfn)
                    s_s.n += 1
                    r_ev[ro] = s_s.n
                    evW = s_s.n

                    def _g(g, c=c, oc=oc, evW=evW, rb=rb):
                        g.wait_ge(s_s.h, evW)
                        s_gd.inc(g.dma_start(
                            out=rb[:, oc * CH:(oc + 1) * CH],
                            in_=xT_d[oc * 128:(oc + 1) * 128, c * CH:(c + 1) * CH],
                            accum_op=OP.add), 16)
                    G(_g)
                def _swoall(s, fns=tuple(wo_defer)):
                    for fn in fns:
                        fn(s)
                S(_swoall)
                r_done = s_gd.n
                state["rdone", c] = r_done

                def _s(s, r_done=r_done, rb=rb):
                    s.wait_ge(s_gd.h, r_done)
                    for dc in range(8):
                        s_s.inc(s.activation(rb16[:, dc * CH:(dc + 1) * CH],
                                             rb[:, dc * CH:(dc + 1) * CH],
                                             AF.Identity))
                S(_s)
                rbc16 = s_s.n

                def _v(v, rbc16=rbc16):
                    v.wait_ge(s_s.h, rbc16)
                    for dc in range(8):
                        s_v.inc(v.tensor_mul(rb2[:, dc * CH:(dc + 1) * CH],
                                             rb16[:, dc * CH:(dc + 1) * CH],
                                             rb16[:, dc * CH:(dc + 1) * CH]))
                V(_v)
                state["rbck", c] = s_v.n
                state["rbck16", c] = rbc16

            def stage2a(c):
                def _t(t, rbck=state["rbck", c], stf=st_free,
                       rb16ck=state["rbck16", c]):
                    t.wait_ge(s_s.h, rb16ck)
                    t.wait_ge(s_v.h, max(rbck, stf))
                    for dc in range(8):
                        s_t.inc(t.matmul(p_st[0:1, :], o128b[:],
                                         rb16[:, dc * CH:(dc + 1) * CH],
                                         start=(dc == 0), stop=(dc == 7)))
                T(_t)
                t0 = s_t.n

                def _v(v, t0=t0):
                    v.wait_ge(s_t.h, t0)
                    s_v.inc(v.tensor_scalar(m_ap, p_st[0:1, :], 1.0 / D, None,
                                            OP.mult))
                V(_v)
                vm = s_v.n

                def _t(t, vm=vm):
                    t.wait_ge(s_v.h, vm)
                    for dc in range(8):
                        s_t.inc(t.matmul(p_st[0:1, :], o128b[:],
                                         rb2[:, dc * CH:(dc + 1) * CH],
                                         start=(dc == 0), stop=(dc == 7)))
                T(_t)
                t1 = s_t.n

                def _v(v, t1=t1):
                    v.wait_ge(s_t.h, t1)
                    s_v.inc(v.tensor_scalar(msq_ap, p_st[0:1, :], 1.0 / D, None,
                                            OP.mult))
                V(_v)
                state["msq", c] = s_v.n

            def stage2b_ln(c):
                def _v(v, mq=state["msq", c]):
                    s_v.inc(v.tensor_mul(m2_ap, m_ap, m_ap))
                    s_v.inc(v.scalar_tensor_tensor(ve_ap, m2_ap, -1.0, msq_ap,
                                                   OP.mult, OP.add))
                    s_v.inc(v.tensor_scalar(ve_ap, ve_ap, EPS_LN, None, OP.add))
                V(_v)
                vs_ = s_v.n

                def _s(s, vs_=vs_, abt=state.get(("abt", c - 1), 0)):
                    s.wait_ge(s_v.h, vs_)
                    if abt:
                        s.wait_ge(s_gd.h, abt)
                    s_s.inc(s.activation(sd_ap, ve_ap, AF.Ln))
                    s.wait_ge(s_s.h, s_s.n)
                    s_s.inc(s.activation(m2_ap, sd_ap, AF.Exp, scale=-0.5))
                S(_s)
                ss_ = s_s.n

                def _v(v, ss_=ss_):
                    v.wait_ge(s_s.h, ss_)
                    s_v.inc(v.scalar_tensor_tensor(ab_ap, m_ap, -1.0, m2_ap,
                                                   OP.mult, OP.mult))
                V(_v)
                vab = s_v.n
                state["vab", c] = vab

                def _t(t, vab=vab, rc=state.get(("recck", c), 0)):
                    t.wait_ge(s_v.h, max(vab, rc))
                    s_t.inc(t.matmul(p_den[0:8, :], o1x128f[0:1, 0:8], m2_ap,
                                     start=True, stop=True))
                T(_t)
                a8 = s_t.n

                def _v(v, a8=a8):
                    v.wait_ge(s_t.h, a8)
                    s_v.inc(v.tensor_copy(aB8sb[:], p_den[0:8, :]))
                V(_v)
                state["a8cp", c] = s_v.n

                if c < 8:
                    def _g(g, vab=vab, tpv_=state.get(("tpaffv", c - 1), 0),
                           tps_=state.get(("tpaffs", c - 1), 0)):
                        g.wait_ge(s_v.h, max(vab, tpv_))
                        if tps_:
                            g.wait_ge(s_s.h, tps_)
                        s_gd.inc(g.dma_start(out=abscr_d[0:1, :], in_=m2_ap), 16)
                        s_gd.inc(g.dma_start(out=abscr_d[1:2, :], in_=ab_ap), 16)
                        g.wait_ge(s_gd.h, s_gd.n)
                        s_gd.inc(g.dma_start(
                            out=abTa[:],
                            in_=abscr_d[0:1, :].rearrange("o (g p) -> (o p) g", p=128),
                            allow_slow_non_contiguous=True), 16)
                        s_gd.inc(g.dma_start(
                            out=abTb[:],
                            in_=abscr_d[1:2, :].rearrange("o (g p) -> (o p) g", p=128),
                            allow_slow_non_contiguous=True), 16)
                    G(_g)
                    state["abt", c] = s_gd.n

            def stage2b_gate(c):
                rb = rbuf[c % 2]
                def _t(t, mq=state["msq", c], vab=state["vab", c],
                       rd=state.get(("rdone", c), 0), stf=st_free, rb=rb):
                    t.wait_ge(s_v.h, max(mq, vab, stf))
                    if rd:
                        t.wait_ge(s_gd.h, rd)
                    for dc in range(8):
                        s_t.inc(t.matmul(p_st[0:8, :],
                                         gwT[:, dc * 8:(dc + 1) * 8],
                                         rb[:, dc * CH:(dc + 1) * CH],
                                         start=(dc == 0), stop=False))
                    s_t.inc(t.matmul(p_st[0:8, :], gws8[:], ab_ap,
                                     start=False, stop=True))
                T(_t)
                tG = s_t.n

                def _v(v, tG=tG, sdma=state.get(("scdma", c - 1), 0)):
                    v.wait_ge(s_t.h, tG)
                    if sdma:
                        v.wait_ge(s_sc.h, sdma)
                    s_v.inc(v.tensor_tensor(sc_sb[:], p_st[0:8, :],
                                            aB8sb[:], OP.mult))
                V(_v)
                state["stfree"] = s_v.n
                state["scv", c] = s_v.n
                scv = s_v.n

                def _sy(sync, c=c, scv=scv):
                    sync.wait_ge(s_v.h, scv)
                    s_sc.inc(sync.dma_start(out=scores_d[:, c * CH:(c + 1) * CH],
                                            in_=sc_sb[:]), 16)
                SY(_sy)
                state["scdma", c] = s_sc.n

            def stage2b_tp(c):
                rb = rbuf[c % 2]
                if c >= 8:
                    return
                tok_defer = []
                for ts in range(4):
                    tb = ts % 2
                    tau = c * 4 + ts
                    for g4 in range(2):
                        pbank = p_tp[g4 % 2]
                        def _t(t, ts=ts, g4=g4, pbank=pbank,
                               tev=tp_evv[g4 % 2],
                               rd=state.get(("rdone", c), 0),
                               tfc=max(x1tf_ck) if g4 == 0 else 0, rb=rb):
                            if rd:
                                t.wait_ge(s_gd.h, rd)
                            if tev:
                                t.wait_ge(s_v.h, tev)
                            if tfc:
                                t.wait_ge(s_sc.h, tfc)
                            for q in range(4):
                                dc = g4 * 4 + q
                                s_t.inc(t.transpose(
                                    pbank[:, q * 128:(q + 1) * 128],
                                    rb[:, dc * CH + ts * 128:dc * CH + (ts + 1) * 128],
                                    idf[:]))
                        T(_t)
                        tT = s_t.n

                        def _v(v, ts=ts, g4=g4, tb=tb, tT=tT, pbank=pbank,
                               abt=state["abt", c], tkb=tokb_ck[tb]):
                            v.wait_ge(s_t.h, tT)
                            v.wait_ge(s_gd.h, abt)
                            if tkb:
                                v.wait_ge(s_s.h, tkb)
                            s_v.inc(v.tensor_scalar(
                                x1tf[tb][:, g4 * 512:(g4 + 1) * 512],
                                pbank[:], abTa[:, ts:ts + 1], abTb[:, ts:ts + 1],
                                OP.mult, OP.add))
                        V(_v)
                        tp_evv[g4 % 2] = s_v.n
                        aff_ck = s_v.n

                        def _sfn(s, ts=ts, g4=g4, tau=tau, tb=tb, aff=aff_ck):
                            s.wait_ge(s_v.h, aff)
                            s_s.raw(s.activation(
                                x1tokb[:, tau * 1024 + g4 * 512:tau * 1024 + (g4 + 1) * 512],
                                x1tf[tb][:, g4 * 512:(g4 + 1) * 512], AF.Identity))
                        tok_defer.append(_sfn)
                        s_s.n += 1
                        tokb_ck[tb] = s_s.n
                    tpv = tp_evv[1]

                    def _sy(sync, tb=tb, tau=tau, tpv=tpv):
                        sync.wait_ge(s_v.h, tpv)
                        s_sc.inc(sync.dma_start(
                            out=acc_d[tau * 128:(tau + 1) * 128, :],
                            in_=x1tf[tb][:]), 16)
                    SY(_sy)
                    x1tf_ck[tb] = s_sc.n

                def _stok(s, fns=tuple(tok_defer)):
                    for fn in fns:
                        fn(s)
                S(_stok)
                state["tpaffv", c] = tp_evv[1]
                state["tpaffs", c] = max(tokb_ck)

            def _v(v, pv=s_v.n):
                v.wait_ge(s_v.h, pv)
                s_v.inc(v.tensor_scalar(onecol[:], iocf[:], 0.0, 1.0,
                                        OP.mult, OP.add))
            V(_v)
            vone = s_v.n

            def _t(t, vone=vone, ldw=lds0):
                t.wait_ge(s_v.h, vone)
                t.wait_ge(s_ld.h, ldw)
                s_t.inc(t.matmul(p_st[0:1, 0:64], onecol[:], gwT[:],
                                 start=True, stop=True))
            T(_t)
            tgw = s_t.n

            def _v(v, tgw=tgw):
                v.wait_ge(s_t.h, tgw)
                s_v.inc(v.tensor_copy(gwsr[:], p_st[0:1, 0:64]))
                v.wait_ge(s_v.h, s_v.n)
                s_v.inc(v.tensor_copy(gws8[:], gwsr[0:1, 0:8]))
                for dc in range(1, 8):
                    v.wait_ge(s_v.h, s_v.n)
                    s_v.inc(v.tensor_add(gws8[:], gws8[:],
                                         gwsr[0:1, dc * 8:(dc + 1) * 8]))
            V(_v)
            vgw = s_v.n

            def _t(t, vgw=vgw):
                t.wait_ge(s_v.h, vgw)
                s_t.inc(t.transpose(p_st[0:8, 0:2], gws8[:], idf[0:1, 0:2]))
            T(_t)
            tgw2 = s_t.n

            def _v(v, tgw2=tgw2):
                v.wait_ge(s_t.h, tgw2)
                s_v.inc(v.tensor_copy(gws_col[:], p_st[0:8, 0:1]))
            V(_v)
            st_free = s_v.n
            den_free = 0
            load_x_chunk(0)
            load_x_chunk(1)
            stage1a(0)
            load_x_chunk(2)
            stage1b_den(0)
            stage1b_attn(0)
            for c in range(1, NCH):
                stage1a(c)
                if c + 2 < NCH:
                    load_x_chunk(c + 2)
                emit_conv(256 + (c - 1) * 16)
                stage2a(c - 1)
                stage2b_ln(c - 1)
                stage1b_den(c)
                stage2b_gate(c - 1)
                st_free = state.get("stfree", 0)
                stage2b_tp(c - 1)
                stage1b_attn(c)
            emit_conv(496)
            stage2a(NCH - 1)
            stage2b_ln(NCH - 1)
            stage2b_gate(NCH - 1)
            stage2b_tp(NCH - 1)
            barrier()

        attn.close()

        if debug_level >= 1:
            def _sy(sync):
                s_sc.inc(sync.dma_start(out=dbg["x1"], in_=acc_d[0:TOK, :]), 16)
                s_sc.inc(sync.dma_start(out=dbg["scores"], in_=scores_d), 16)
            SY(_sy)
            barrier()
        if debug_level == 1:
            def _sy(sync):
                sync.wait_ge(s_sc.h, s_sc.n)
            SY(_sy)
            return nc, dbg

        with ExitStack() as p4:
            prb = sbuf(p4, "prb", [128, 512])
            Ee = sbuf(p4, "Ee", [128, 512])
            rc16 = sbuf(p4, "rc16", [16, 512])
            m512 = sbuf(p4, "m512", [128, 512])
            cnt128 = sbuf(p4, "cnt128", [128, 2])
            lohi = sbuf(p4, "lohi", [16, 6])
            selu = sbuf(p4, "selu", [8, 1], U32)
            thcol = sbuf(p4, "thcol", [128, 1])
            thvec = sbuf(p4, "thvec", [128, 1])
            throw = sbuf(p4, "throw", [1, 8])
            scmy = sbuf(p4, "scmy", [8, TOK])
            etok = sbuf(p4, "etok", [128, 8])
            rs128 = sbuf(p4, "rs128", [128, 1])
            Rp = [psum(p4, f"p4_r{i}") for i in range(2)]

            def VS(v):
                v.wait_ge(s_v.h, s_v.n)

            def _sy(sync, scw=s_sc.n):
                sync.wait_ge(s_sc.h, scw)
                s_ld.inc(sync.dma_start(
                    out=prb[:], in_=scores_d.rearrange("e (b j) -> (e b) j", j=512)), 16)
                s_ld.inc(sync.dma_start(out=scmy[:], in_=scores_d[:, 0:TOK]), 16)
            SY(_sy)
            ld4 = s_ld.n

            def _s(s, ld4=ld4):
                s.wait_ge(s_ld.h, ld4)
                s_s.inc(s.activation(Ee[:], prb[:], AF.Exp))
            S(_s)
            eck = s_s.n

            def _t(t, eck=eck):
                t.wait_ge(s_s.h, eck)
                s_t.inc(t.matmul(Rp[0][0:16, :], iB128_16[:], Ee[:],
                                 start=True, stop=True))
            T(_t)
            tck = s_t.n

            def _v(v, tck=tck):
                v.wait_ge(s_t.h, tck)
                s_v.inc(v.reciprocal(rc16[:], Rp[0][0:16, :]))
            V(_v)
            vck = s_v.n

            def _t(t, vck=vck):
                t.wait_ge(s_v.h, vck)
                s_t.inc(t.matmul(Rp[1][:], iB16_128[:], rc16[:],
                                 start=True, stop=True))
            T(_t)
            tck = s_t.n

            def _v(v, tck=tck):
                v.wait_ge(s_t.h, tck)
                s_v.inc(v.tensor_tensor(prb[:], Ee[:], Rp[1][:], OP.mult))
                s_v.inc(v.memset(lohi[:], 0.0))
                s_v.inc(v.memset(thvec[:], 0.0))
                VS(v)
                s_v.inc(v.tensor_scalar(lohi[0:8, 1:2], lohi[0:8, 1:2], 1.0,
                                        None, OP.add))
            V(_v)

            for it in range(NBIS):
                def _v(v):
                    VS(v)
                    s_v.inc(v.tensor_add(lohi[0:8, 2:3], lohi[0:8, 0:1],
                                         lohi[0:8, 1:2]))
                    VS(v)
                    s_v.inc(v.tensor_scalar(lohi[0:8, 2:3], lohi[0:8, 2:3], 0.5,
                                            None, OP.mult))
                V(_v)
                vth = s_v.n

                def _t(t, vth=vth):
                    t.wait_ge(s_v.h, vth)
                    s_t.inc(t.matmul(Rp[0][:, 0:2], ind8p[:], lohi[0:16, 2:4],
                                     start=True, stop=True))
                T(_t)
                tth = s_t.n

                def _v(v, tth=tth):
                    v.wait_ge(s_t.h, tth)
                    s_v.inc(v.tensor_copy(thcol[:], Rp[0][:, 0:1]))
                    VS(v)
                    s_v.inc(v.tensor_scalar(m512[:], prb[:], thcol[:, 0:1],
                                            None, OP.is_ge))
                    s_v.inc(v.reduce_sum(cnt128[:, 0:1], m512[:], axis=AX.X))
                V(_v)
                vcnt = s_v.n

                def _t(t, vcnt=vcnt):
                    t.wait_ge(s_v.h, vcnt)
                    s_t.inc(t.matmul(Rp[1][0:8, 0:2], i128_8[:], cnt128[:],
                                     start=True, stop=True))
                T(_t)
                tcnt = s_t.n

                def _v(v, tcnt=tcnt):
                    v.wait_ge(s_t.h, tcnt)
                    s_v.inc(v.tensor_scalar(lohi[0:8, 3:4], Rp[1][0:8, 0:1],
                                            2048.0, None, OP.is_gt))
                    VS(v)
                    s_v.inc(v.tensor_copy(selu[:], lohi[0:8, 3:4]))
                    VS(v)
                    s_v.inc(v.select(lohi[0:8, 4:5], selu[:], lohi[0:8, 2:3],
                                     lohi[0:8, 0:1]))
                    s_v.inc(v.select(lohi[0:8, 5:6], selu[:], lohi[0:8, 1:2],
                                     lohi[0:8, 2:3]))
                    VS(v)
                    s_v.inc(v.tensor_copy(lohi[0:8, 0:1], lohi[0:8, 4:5]))
                    s_v.inc(v.tensor_copy(lohi[0:8, 1:2], lohi[0:8, 5:6]))
                V(_v)
            bis_v = s_v.n

            def _v(v, bis_v=bis_v):
                VS(v)
                s_v.inc(v.tensor_copy(thvec[0:8, 0:1], lohi[0:8, 1:2]))
            V(_v)
            vck = s_v.n

            def _t(t, vck=vck):
                t.wait_ge(s_v.h, vck)
                s_t.inc(t.transpose(Rp[0][0:1, 0:128], thvec[:], idf[:]))
            T(_t)
            tck = s_t.n

            def _v(v, tck=tck):
                v.wait_ge(s_t.h, tck)
                s_v.inc(v.tensor_copy(throw[:], Rp[0][0:1, 0:8]))
            V(_v)
            vck = s_v.n

            def _t(t, vck=vck):
                t.wait_ge(s_v.h, vck)
                s_t.inc(t.matmul(Rp[1][:, 0:8], o1x128f[:], throw[:],
                                 start=True, stop=True))
            T(_t)
            tck = s_t.n

            def _v(v, tck=tck):
                v.wait_ge(s_t.h, tck)
                s_v.inc(v.tensor_copy(thB8[:], Rp[1][:, 0:8]))
            V(_v)

            ld4b = ld4
            etok_defer = []
            for tau in range(MYT):
                rp = tau % 2
                def _t(t, tau=tau, rp=rp, ld4b=ld4b, pv=s_v.n):
                    t.wait_ge(s_ld.h, ld4b)
                    t.wait_ge(s_v.h, pv)
                    s_t.inc(t.transpose(Rp[rp][:, 0:8],
                                        scmy[:, tau * 128:(tau + 1) * 128],
                                        idf[0:8, 0:8]))
                T(_t)
                tck = s_t.n

                def _sfn(s, rp=rp, tck=tck, pv=s_v.n):
                    s.wait_ge(s_t.h, tck)
                    if pv:
                        s.wait_ge(s_v.h, pv)
                    s_s.raw(s.activation(etok[:], Rp[rp][:, 0:8], AF.Exp))
                etok_defer.append(_sfn)
                s_s.n += 1
                sck = s_s.n

                def _v(v, tau=tau, sck=sck):
                    v.wait_ge(s_s.h, sck)
                    s_v.inc(v.reduce_sum(rs128[:], etok[:], axis=AX.X))
                    VS(v)
                    s_v.inc(v.reciprocal(rs128[:], rs128[:]))
                    VS(v)
                    s_v.inc(v.tensor_scalar(ptokf[:, tau * 8:(tau + 1) * 8],
                                            etok[:], rs128[:, 0:1], None,
                                            OP.mult))
                    VS(v)
                    s_v.inc(v.tensor_copy(ptokb[:, tau * 8:(tau + 1) * 8],
                                          ptokf[:, tau * 8:(tau + 1) * 8]))
                V(_v)

            def _setall(s, fns=tuple(etok_defer)):
                for fn in fns:
                    fn(s)
            S(_setall)
            barrier()

            if debug_level >= 2:
                def _sy(sync):
                    s_sc.inc(sync.dma_start(out=dbg["theta"], in_=lohi[0:8, 1:2]), 16)
                SY(_sy)

        if stop_stage == "p4":
            def _fin4(eng):
                eng.wait_ge(s_sc.h, s_sc.n)
                eng.wait_ge(s_v.h, s_v.n)
            SY(_fin4)
            return nc, dbg

        with ExitStack() as p45:
            maskb = sbuf(p45, "maskb", [128, MYT * 8], BF16)
            masku = sbuf(p45, "masku", [128, 8], U32)
            tot8b = sbuf(p45, "tot8b", [8, 32], BF16)
            totTb = sbuf(p45, "totTb", [32, 8], BF16)
            offs_sb = sbuf(p45, "offs_sb", [32, 8])
            offs1 = sbuf(p45, "offs1", [1, 256])
            meta_sb = sbuf(p45, "meta_sb", [128, 10], BF16)
            Pb = [sbuf(p45, f"Pb{i}", [128, 128], BF16) for i in range(2)]
            cmpx = [sbuf(p45, f"cmpx{i}", [128, 1040], BF16) for i in range(4)]

            PS = [psum(p45, f"p45_{i}") for i in range(6)]

            def VS(v):
                v.wait_ge(s_v.h, s_v.n)
            for tau in range(MYT):
                rp = tau % 2
                def _v(v, tau=tau, tpv=s_t.n):
                    if tpv:
                        v.wait_ge(s_t.h, tpv)
                    s_v.inc(v.tensor_tensor(masku[:], ptokf[:, tau * 8:(tau + 1) * 8],
                                            thB8[:], OP.is_ge))
                    VS(v)
                    s_v.inc(v.tensor_copy(maskb[:, tau * 8:(tau + 1) * 8],
                                          masku[:]))
                V(_v)
                vm = s_v.n

                def _t(t, tau=tau, rp=rp, vm=vm, pfree=s_v.n):
                    t.wait_ge(s_v.h, max(vm, pfree))
                    s_t.inc(t.matmul(PS[rp][:, 0:8], tri[:],
                                     maskb[:, tau * 8:(tau + 1) * 8],
                                     start=True, stop=True))
                T(_t)
                tp = s_t.n

                def _v(v, tau=tau, rp=rp, tp=tp):
                    v.wait_ge(s_t.h, tp)
                    s_v.inc(v.tensor_scalar(slotv[:, tau * 8:(tau + 1) * 8],
                                            PS[rp][:, 0:8], 1.0, None,
                                            OP.subtract))
                    VS(v)
                    s_v.inc(v.select(slotv[:, tau * 8:(tau + 1) * 8], masku[:],
                                     slotv[:, tau * 8:(tau + 1) * 8], big8[:]))
                V(_v)
            mask_done = s_v.n

            def _t(t, mask_done=mask_done):
                t.wait_ge(s_v.h, mask_done)
                for tau in range(MYT):
                    s_t.inc(t.matmul(PS[2][0:8, tau:tau + 1],
                                     maskb[:, tau * 8:(tau + 1) * 8],
                                     o128b[:], start=(tau == 0),
                                     stop=(tau == MYT - 1)))
            T(_t)
            tt = s_t.n

            def _v(v, tt=tt):
                v.wait_ge(s_t.h, tt)
                s_v.inc(v.tensor_copy(tot8b[:], PS[2][0:8, 0:32]))
            V(_v)
            vt = s_v.n

            def _t(t, vt=vt):
                t.wait_ge(s_v.h, vt)
                s_t.inc(t.transpose(PS[3][:].bitcast(BF16)[0:32, 0:8],
                                    tot8b[:], idb[0:8, 0:8]))
            T(_t)
            tt = s_t.n

            def _v(v, tt=tt):
                v.wait_ge(s_t.h, tt)
                s_v.inc(v.tensor_copy(totTb[:], PS[3][:].bitcast(BF16)[0:32, 0:8]))
            V(_v)
            vt = s_v.n

            def _t(t, vt=vt):
                t.wait_ge(s_v.h, vt)
                s_t.inc(t.matmul(PS[2][0:32, 0:8], tis[:], totTb[:],
                                 start=True, stop=True))
                s_t.inc(t.matmul(PS[3][0:32, 0:8], o32sq[:], totTb[:],
                                 start=True, stop=True))
            T(_t)
            tt = s_t.n

            def _v(v, tt=tt):
                v.wait_ge(s_t.h, tt)
                s_v.inc(v.tensor_copy(offs_sb[:], PS[2][0:32, 0:8]))
                s_v.inc(v.tensor_copy(counts1[:], PS[3][0:1, 0:8]))
            V(_v)
            vt = s_v.n

            def _t(t, vt=vt):
                t.wait_ge(s_v.h, vt)
                s_t.inc(t.matmul(PS[4][:, 0:8], o1x128f[:], counts1[:],
                                 start=True, stop=True))
            T(_t)
            tt = s_t.n

            def _v(v, tt=tt):
                v.wait_ge(s_t.h, tt)
                s_v.inc(v.tensor_copy(countB[:], PS[4][:, 0:8]))
            V(_v)
            vcb = s_v.n

            def _sy(sync, vcb=vcb):
                sync.wait_ge(s_v.h, vcb)
                s_ld.inc(sync.dma_start(out=offs_d, in_=offs_sb[:]), 16)
                sync.wait_ge(s_ld.h, s_ld.n)
                s_ld.inc(sync.dma_start(out=offs1[:],
                                        in_=offs_d.rearrange("(x a) b -> x (a b)", x=1)), 16)
            SY(_sy)
            ofld = s_ld.n

            def _v(v, ofld=ofld):
                v.wait_ge(s_ld.h, ofld)
                s_v.inc(v.tensor_scalar(offsc[:], offs1[:], 2080.0, None, OP.mult))
                VS(v)
                s_v.inc(v.tensor_copy(offbu[:], offsc[:]))
                VS(v)
            V(_v)
            offs_ready = s_v.n

            if debug_level >= 2:
                def _sy(sync, offs_ready=offs_ready):
                    sync.wait_ge(s_v.h, offs_ready)
                    s_sc.inc(sync.dma_start(out=dbg["counts"], in_=counts1[:]), 16)
                    s_sc.inc(sync.dma_start(out=dbg["offs"], in_=offs_sb[:]), 16)
                SY(_sy)

            cmp_ck = [(None, 0)] * 4
            ps_ev = [(0, 0), (0, 0)]
            meta_ck = [0]
            wq_items = []
            cx_defer = []
            for tau in range(MYT):
                def _v(v, tau=tau, mck=meta_ck[0]):
                    if mck:
                        v.wait_ge(s_t.h, mck)
                    s_v.inc(v.tensor_copy(meta_sb[:, 0:8],
                                          ptokb[:, tau * 8:(tau + 1) * 8]))
                    s_v.inc(v.tensor_scalar(meta_sb[:, 8:9], iocb[:], 0.0,
                                            float(tau), OP.mult, OP.add))
                    s_v.inc(v.tensor_copy(meta_sb[:, 9:10], iocb[:]))
                V(_v)
                vmeta = s_v.n
                for e in range(E):
                    k = (tau * E + e) % 2
                    kb4 = (tau * E + e) % 4
                    def _v(v, tau=tau, e=e, k=k, pck=s_t.n):
                        if pck:
                            v.wait_ge(s_t.h, pck)
                        VS(v)
                        s_v.inc(v.tensor_scalar(Pb[k][:], iosq[:],
                                                slotv[:, tau * 8 + e:tau * 8 + e + 1],
                                                None, OP.is_equal))
                    V(_v)
                    vP = s_v.n

                    def _t(t, tau=tau, e=e, k=k, kb4=kb4, vP=vP, vmeta=vmeta,
                           exv=cmp_ck[kb4], pse=ps_ev[(tau * E + e) % 2]):
                        t.wait_ge(s_v.h, max(vP, vmeta, pse[1]))
                        if pse[0]:
                            t.wait_ge(s_s.h, pse[0])
                        if exv[0] is not None:
                            t.wait_ge(exv[0].h, exv[1])
                        s_t.inc(t.matmul(PS[kb4 % 2 * 2 + 0][:], Pb[k][:],
                                         x1tokb[:, tau * 1024:tau * 1024 + 512],
                                         start=True, stop=True))
                        s_t.inc(t.matmul(PS[kb4 % 2 * 2 + 1][:], Pb[k][:],
                                         x1tokb[:, tau * 1024 + 512:(tau + 1) * 1024],
                                         start=True, stop=True))
                        s_t.inc(t.matmul(PS[4 + kb4 % 2][:, 0:10], Pb[k][:],
                                         meta_sb[:], start=True, stop=True))
                    T(_t)
                    meta_ck[0] = s_t.n
                    tc = s_t.n

                    def _sfn(s, kb4=kb4, tc=tc):
                        s.wait_ge(s_t.h, tc)
                        s_s.raw(s.activation(cmpx[kb4][:, 0:512],
                                             PS[kb4 % 2 * 2 + 0][:], AF.Copy))
                    cx_defer.append(_sfn)
                    s_s.n += 1
                    sc_ = s_s.n

                    def _v(v, kb4=kb4, tc=tc):
                        v.wait_ge(s_t.h, tc)
                        s_v.inc(v.tensor_copy(cmpx[kb4][:, 512:1024],
                                              PS[kb4 % 2 * 2 + 1][:]))
                        s_v.inc(v.tensor_copy(cmpx[kb4][:, 1024:1034],
                                              PS[4 + kb4 % 2][:, 0:10]))
                    V(_v)
                    vc = s_v.n

                    wq_items.append((tau, e, kb4, sc_, vc, 16 * tau))
                    s_wr[e].n += 16
                    cmp_ck[kb4] = (s_wr[e], s_wr[e].n)
                    ps_ev[(tau * E + e) % 2] = (sc_, vc)
            def _scxall(s, fns=tuple(cx_defer)):
                for fn in fns:
                    fn(s)
            S(_scxall)

            def _sqall(sync, items=tuple(wq_items), offs_ready=offs_ready):
                regx = sync.register("offregx").__enter__()
                rvx = make_scalar_value(regx, min_val=0,
                                        max_val=(XROWS - 128) * 2080)
                xin_flat = [xin_ds[e].rearrange("a b -> (a b)").bitcast(mybir.dt.uint8)
                            for e in range(E)]
                for (tau, e, kb4, sc_, vc, wr_prev) in items:
                    sync.wait_ge(s_v.h, max(vc, offs_ready))
                    sync.wait_ge(s_s.h, sc_)
                    if wr_prev:
                        sync.wait_ge(s_wr[e].h, wr_prev)
                    sync.reg_load(regx, offbu[0:1, tau * 8 + e:tau * 8 + e + 1])
                    sync.dma_start(
                        out=xin_flat[e][DynSlice(rvx, 128 * 2080)]
                            .rearrange("(a b) -> a b", b=2080),
                        in_=cmpx[kb4][:].bitcast(mybir.dt.uint8)).then_inc(s_wr[e].h, 16)
            SY(_sqall)
            barrier()

            if debug_level >= 2:
                def _sy(sync):
                    s_sc.inc(sync.dma_start(out=dbg["xin0"],
                                            in_=xin_ds[0].bitcast(U16)), 16)
                    sync.wait_ge(s_sc.h, s_sc.n)
                SY(_sy)
                barrier()

        if stop_stage == "p45":
            def _fin45(eng):
                for sw in s_wr:
                    eng.wait_ge(sw.h, sw.n)
            SY(_fin45)
            return nc, dbg

        mid.close()

        with ExitStack() as ffn:
            xinT2 = [sbuf(ffn, f"xinT{i}", [128, 8 * CAP], BF16) for i in range(2)]
            w1sb = [sbuf(ffn, f"w1sb{i}", [128, 1024], BF16) for i in range(2)]
            w2sb = sbuf(ffn, "w2sb", [128, 32 * 1024], BF16)
            hsb = sbuf(ffn, "hsb", [128, 32 * CAP], BF16)
            gt2 = [sbuf(ffn, f"gt2_{i}", [128, 90], BF16) for i in range(2)]
            gtf2 = [sbuf(ffn, f"gtf2_{i}", [128, 9]) for i in range(2)]
            idxu = sbuf(ffn, "idxu", [128, 9], U32)
            scr1 = sbuf(ffn, "scr1", [128, 1])
            scr2 = sbuf(ffn, "scr2", [128, 1], U32)
            scr3 = sbuf(ffn, "scr3", [128, 1])
            ysb = [sbuf(ffn, f"ysb{i}", [128, 1024]) for i in range(2)]
            PP = [psum(ffn, f"ffn_{i}") for i in range(8)]

            mm2_done = 0
            w1_ck = [0, 0]
            gelu_ck = [0, 0]
            y_ev = [0, 0]
            cat_ck = [0, 0]
            mm1_done = [0, 0]
            rb_cks = [0, 0]

            def emit_readback(e):
                xt = xinT2[e % 2]
                def _sy(sync, e=e, wr=s_wr[e].n, tprev=mm1_done[e % 2],
                        xt=xt, yprev=max(y_ev)):
                    sync.wait_ge(s_wr[e].h, wr)
                    if tprev:
                        sync.wait_ge(s_t.h, tprev)
                    for dc in range(8):
                        s_ld.inc(sync.dma_start_transpose(
                            xt[:, dc * CAP:(dc + 1) * CAP],
                            xin_ds[e][0:CAP, dc * 128:(dc + 1) * 128]), 16)
                    if yprev:
                        sync.wait_ge(s_v.h, yprev)
                    s_ld.inc(sync.dma_start(
                        out=gt2[e % 2][:].rearrange("p (sg q) -> p sg q", sg=9),
                        in_=xin_ds[e][0:CAP, 1024:1034]
                            .rearrange("(sg p) q -> p sg q", p=128)), 16)
                SY(_sy)
                rb_cks[e % 2] = s_ld.n

            def emit_w1(e, f):
                pw = f % 2
                def _g(g, e=e, f=f, pw=pw, used=w1_ck[pw]):
                    if used:
                        g.wait_ge(s_t.h, used)
                    s_gd.inc(g.dma_start(
                        out=w1sb[pw][:].rearrange("p (dc q) -> p dc q", dc=8),
                        in_=w1b_d[e, f].rearrange("(dc p) q -> p dc q", p=128)), 16)
                G(_g)
                return s_gd.n

            emit_readback(0)
            w1_pre = [emit_w1(0, 0), emit_w1(0, 1)]
            for e in range(E):
                xinT = xinT2[e % 2]
                gt_sb = gt2[e % 2]
                rb_ck = rb_cks[e % 2]

                def _g(sync, e=e, md=mm2_done):
                    if md:
                        sync.wait_ge(s_t.h, md)
                    s_ld.inc(sync.dma_start(
                        out=w2sb[:].rearrange("p (fc q) -> p fc q", fc=32),
                        in_=w2b_d[e].rearrange("fc p q -> p fc q")), 16)
                SY(_g)
                w2_ck = s_ld.n

                def _v(v, e=e, rb_ck=rb_ck, yprev=max(y_ev), gt_sb=gt_sb):
                    v.wait_ge(s_ld.h, rb_ck)
                    if yprev:
                        v.wait_ge(s_v.h, yprev)
                    s_v.inc(v.tensor_copy(
                        gtf2[e % 2][:],
                        gt_sb[:].rearrange("p (sg q) -> p sg q", q=10)[:, :, e]))
                V(_v)

                for sg in range(9):
                    def _v(v, e=e, sg=sg, rb_ck=rb_ck, catprev=s_cat.n,
                           gt_sb=gt_sb):
                        v.wait_ge(s_ld.h, rb_ck)
                        if catprev:
                            v.wait_ge(s_cat.h, catprev)
                        s_v.inc(v.scalar_tensor_tensor(
                            scr1[:], gt_sb[:, sg * 10 + 8:sg * 10 + 9], 128.0,
                            gt_sb[:, sg * 10 + 9:sg * 10 + 10], OP.mult, OP.add))
                        s_v.inc(v.tensor_scalar(scr3[:], countB[:, e:e + 1],
                                                float(sg * 128), None, OP.subtract))
                        v.wait_ge(s_v.h, s_v.n)
                        s_v.inc(v.tensor_tensor(scr2[:], iocf[:], scr3[:],
                                                OP.is_lt))
                        v.wait_ge(s_v.h, s_v.n)
                        s_v.inc(v.select(scr1[:], scr2[:], scr1[:], trshf[:]))
                        v.wait_ge(s_v.h, s_v.n)
                        s_v.inc(v.tensor_copy(idxu[:, sg:sg + 1], scr1[:]))
                    V(_v)
                gidx_ck = s_v.n

                gelu_defer = []
                for f in range(32):
                    pw = f % 2
                    grp = (f % 2) * 3
                    ldw1 = w1_pre[pw]

                    def _t(t, e=e, f=f, pw=pw, grp=grp, ldw1=ldw1,
                           rb_ck=rb_ck, gck=gelu_ck[pw],
                           yprev=max(y_ev) if f == 0 else 0):
                        t.wait_ge(s_gd.h, ldw1)
                        t.wait_ge(s_ld.h, rb_ck)
                        if yprev:
                            t.wait_ge(s_v.h, yprev)
                        if gck:
                            t.wait_ge(s_s.h, gck)
                        for dc in range(8):
                            lhs = w1sb[pw][:, dc * 128:(dc + 1) * 128]
                            s_t.inc(t.matmul(PP[grp + 0][:], lhs,
                                             xinT[:, dc * CAP:dc * CAP + 512],
                                             start=(dc == 0), stop=(dc == 7)))
                            s_t.inc(t.matmul(PP[grp + 1][:], lhs,
                                             xinT[:, dc * CAP + 512:dc * CAP + 1024],
                                             start=(dc == 0), stop=(dc == 7)))
                            s_t.inc(t.matmul(PP[grp + 2][:, 0:128], lhs,
                                             xinT[:, dc * CAP + 1024:dc * CAP + CAP],
                                             start=(dc == 0), stop=(dc == 7)))
                    T(_t)
                    w1_ck[pw] = s_t.n
                    t1 = s_t.n
                    if f + 2 < 32:
                        w1_pre[pw] = emit_w1(e, f + 2)
                    elif e + 1 < E:
                        w1_pre[pw] = emit_w1(e + 1, f + 2 - 32)

                    def _gfn(s, f=f, grp=grp, t1=t1):
                        s.wait_ge(s_t.h, t1)
                        s_s.raw(s.activation(hsb[:, f * CAP:f * CAP + 512],
                                             PP[grp + 0][:], AF.Gelu))
                        s_s.raw(s.activation(hsb[:, f * CAP + 512:f * CAP + 1024],
                                             PP[grp + 1][:], AF.Gelu))
                        s_s.raw(s.activation(hsb[:, f * CAP + 1024:f * CAP + CAP],
                                             PP[grp + 2][:, 0:128], AF.Gelu))
                    gelu_defer.append(_gfn)
                    s_s.n += 3
                    gelu_ck[pw] = s_s.n
                def _sall(s, fns=tuple(gelu_defer)):
                    for fn in fns:
                        fn(s)
                S(_sall)
                h_done = s_s.n
                mm1_done[e % 2] = s_t.n
                if e + 1 < E:
                    emit_readback(e + 1)

                for sg in range(9):
                    yb = sg % 2
                    py0, py1 = (PP[6], PP[7]) if yb == 0 else (PP[0], PP[1])
                    def _t(t, e=e, sg=sg, yb=yb, h_done=h_done, w2_ck=w2_ck,
                           yev=y_ev[yb], gidx_ck=gidx_ck, py0=py0, py1=py1):
                        t.wait_ge(s_s.h, h_done)
                        t.wait_ge(s_ld.h, w2_ck)
                        t.wait_ge(s_v.h, gidx_ck)
                        if yev:
                            t.wait_ge(s_v.h, yev)
                        for f in range(32):
                            lhs = hsb[:, f * CAP + sg * 128:f * CAP + (sg + 1) * 128]
                            s_t.inc(t.matmul(py0[:], lhs,
                                             w2sb[:, f * 1024:f * 1024 + 512],
                                             start=(f == 0), stop=(f == 31)))
                            s_t.inc(t.matmul(py1[:], lhs,
                                             w2sb[:, f * 1024 + 512:(f + 1) * 1024],
                                             start=(f == 0), stop=(f == 31)))
                    T(_t)
                    t2 = s_t.n

                    def _v(v, e=e, sg=sg, yb=yb, t2=t2, cck=cat_ck[yb],
                           py0=py0, py1=py1):
                        v.wait_ge(s_t.h, t2)
                        if cck:
                            v.wait_ge(s_cat.h, cck)
                        gcol = gtf2[e % 2][:, sg:sg + 1]
                        s_v.inc(v.tensor_scalar(ysb[yb][:, 0:512], py0[:],
                                                gcol, None, OP.mult))
                        s_v.inc(v.tensor_scalar(ysb[yb][:, 512:1024], py1[:],
                                                gcol, None, OP.mult))
                    V(_v)
                    y_ev[yb] = s_v.n
                    yv = s_v.n

                    def _g(g, e=e, sg=sg, yb=yb, yv=yv,
                           eprev=(s_cat.n if sg == 0 else 0)):
                        g.wait_ge(s_v.h, yv)
                        if sg == 0 and eprev:
                            g.wait_ge(s_cat.h, eprev)
                        s_cat.inc(g.indirect_dma_start(
                            out=acc_d, out_offset=IndirectOffsetOnAxis(idxu[:, sg:sg + 1], 0),
                            in_=ysb[yb][:], in_offset=None,
                            compute_op=OP.add), 16)
                    G(_g)
                    cat_ck[yb] = s_cat.n
                mm2_done = s_t.n
            barrier()

        if stop_stage == "ffn":
            def _finf(eng):
                eng.wait_ge(s_cat.h, s_cat.n)
            SY(_finf)
            return nc, dbg

        with ExitStack() as ln2:
            acct = [sbuf(ln2, f"acct{i}", [128, 1024]) for i in range(2)]
            outt = [sbuf(ln2, f"outt{i}", [128, 1024]) for i in range(2)]
            sqs = sbuf(ln2, "sqs", [128, 1024])
            stt = sbuf(ln2, "stt", [128, 16])
            out_ck = [0, 0]
            ln_v = [0, 0]
            ln_defer = []
            for tau in range(MYT):
                tb = tau % 2
                ba = tb * 8
                def _sq(sync, tau=tau, tb=tb, cat_all=s_cat.n, oc=out_ck[tb]):
                    sync.wait_ge(s_cat.h, cat_all)
                    if oc:
                        sync.wait_ge(s_out.h, oc)
                    s_sc.inc(sync.dma_start(out=acct[tb][:],
                                            in_=acc_d[tau * 128:(tau + 1) * 128, :]), 16)
                SY(_sq)
                lck = s_sc.n

                def _sfn(s, tb=tb, ba=ba, lck=lck, lv=ln_v[tb]):
                    s.wait_ge(s_sc.h, lck)
                    if lv:
                        s.wait_ge(s_v.h, lv)
                    s_s.raw(s.activation(sqs[:], acct[tb][:], AF.Square,
                                         accum_out=stt[:, ba + 1:ba + 2]))
                ln_defer.append(_sfn)
                s_s.n += 1
                sck = s_s.n

                def _v(v, tb=tb, ba=ba, lck=lck, sck=sck):
                    v.wait_ge(s_sc.h, lck)
                    v.wait_ge(s_s.h, sck)
                    s_v.inc(v.reduce_sum(stt[:, ba:ba + 1], acct[tb][:], axis=AX.X))
                    v.wait_ge(s_v.h, s_v.n)
                    s_v.inc(v.tensor_scalar(stt[:, ba + 2:ba + 3], stt[:, ba:ba + 1],
                                            1.0 / D, None, OP.mult))
                    s_v.inc(v.tensor_scalar(stt[:, ba + 3:ba + 4], stt[:, ba + 1:ba + 2],
                                            1.0 / D, None, OP.mult))
                    v.wait_ge(s_v.h, s_v.n)
                    s_v.inc(v.tensor_scalar(stt[:, ba + 4:ba + 5], stt[:, ba + 2:ba + 3],
                                            stt[:, ba + 2:ba + 3], None, OP.mult))
                    v.wait_ge(s_v.h, s_v.n)
                    s_v.inc(v.scalar_tensor_tensor(stt[:, ba + 5:ba + 6], stt[:, ba + 4:ba + 5],
                                                   -1.0, stt[:, ba + 3:ba + 4],
                                                   OP.mult, OP.add))
                    v.wait_ge(s_v.h, s_v.n)
                    s_v.inc(v.tensor_scalar(stt[:, ba + 5:ba + 6], stt[:, ba + 5:ba + 6],
                                            EPS_LN, None, OP.add))
                V(_v)
                vck = s_v.n

                def _sfn2(s, ba=ba, vck=vck, lnck=0):
                    s.wait_ge(s_v.h, vck)
                    s_s.raw(s.activation(stt[:, ba + 6:ba + 7], stt[:, ba + 5:ba + 6],
                                         AF.Ln))
                    s.wait_ge(s_s.h, lnck)
                    s_s.raw(s.activation(stt[:, ba + 7:ba + 8], stt[:, ba + 6:ba + 7],
                                         AF.Exp, scale=-0.5))
                s_s.n += 1
                lnck = s_s.n
                _sfn2.__defaults__ = (ba, vck, lnck)
                ln_defer.append(_sfn2)
                s_s.n += 1
                sck = s_s.n

                def _v(v, tb=tb, ba=ba, sck=sck, oc=out_ck[tb]):
                    v.wait_ge(s_s.h, sck)
                    if oc:
                        v.wait_ge(s_out.h, oc)
                    s_v.inc(v.tensor_scalar(outt[tb][:], acct[tb][:],
                                            stt[:, ba + 2:ba + 3], stt[:, ba + 7:ba + 8],
                                            OP.subtract, OP.mult))
                V(_v)
                ln_v[tb] = s_v.n
                vck = s_v.n

                def _gq(g, tau=tau, tb=tb, vck=vck):
                    g.wait_ge(s_v.h, vck)
                    s_out.inc(g.dma_start(out=out_d[tau * 128:(tau + 1) * 128, :],
                                          in_=outt[tb][:]), 16)
                G(_gq)
                out_ck[tb] = s_out.n

            def _slnall(s, fns=tuple(ln_defer)):
                for fn in fns:
                    fn(s)
            S(_slnall)

            def _fin(eng):
                eng.wait_ge(s_out.h, s_out.n)
            SY(_fin)
            S(_fin)

    return nc, dbg



_PROG_CACHE = {}


def _get_program(debug_level=0, stop_stage=None):
    key = (debug_level, stop_stage)
    if key not in _PROG_CACHE:
        _PROG_CACHE[key] = build_program(debug_level, stop_stage)
    return _PROG_CACHE[key]


def _consts():
    f32, bf16 = np.float32, ml_dtypes.bfloat16
    c = {}
    c["ident_f"] = np.eye(128, dtype=f32)
    c["ident_b"] = np.eye(128, dtype=f32).astype(bf16)
    c["tri128_b"] = np.triu(np.ones((128, 128), f32)).astype(bf16)
    c["triS32_b"] = np.triu(np.ones((32, 32), f32), 1).astype(bf16)
    c["iosq_f"] = np.tile(np.arange(128, dtype=f32), (128, 1))
    c["iocol_f"] = np.arange(128, dtype=f32)[:, None]
    c["iocol_b"] = np.arange(128, dtype=f32)[:, None].astype(bf16)
    c["trash_f"] = (4096.0 + np.arange(128, dtype=f32))[:, None]
    c["big8_f"] = np.full((128, 8), 50000.0, f32)
    c["ones128_1_b"] = np.ones((128, 1), f32).astype(bf16)
    c["ones1_128_f"] = np.ones((1, 128), f32)
    p = np.arange(128)
    c["ind128_8_f"] = (p[:, None] // 16 == np.arange(8)[None, :]).astype(f32)
    c["ind8_128_f"] = (np.arange(8)[:, None] == p[None, :] // 16).astype(f32)
    c["ind8p_f"] = np.zeros((16, 128), f32)
    c["ind8p_f"][:8] = c["ind8_128_f"]
    c["ones32_32_b"] = np.ones((32, 32), f32).astype(bf16)
    c["indB128_16_f"] = (p[:, None] % 16 == np.arange(16)[None, :]).astype(f32)
    c["indB16_128_f"] = (np.arange(16)[:, None] == p[None, :] % 16).astype(f32)
    ip = np.zeros((16, 8 * 128), f32)
    for c8 in range(8):
        for pp in range(128):
            ip[2 * c8 + (1 if pp >= 64 else 0), c8 * 128 + pp] = 1.0
    c["indPair_f"] = ip.astype(bf16)
    return c


def kernel(L0_attn_x, L0_attn_Wq, L0_attn_bq, L0_attn_Wk, L0_attn_bk,
           L0_attn_Wv, L0_attn_bv, L0_attn_Wo, L0_attn_bo, L0_attn_P,
           L0_ffn_gate_w, L0_ffn_w1, L0_ffn_w2, _debug_level=0, _stop=None):
    import jax
    f32 = np.float32
    x = np.asarray(L0_attn_x, f32)
    Wq, Wk, Wv, Wo = (np.asarray(a, f32) for a in
                      (L0_attn_Wq, L0_attn_Wk, L0_attn_Wv, L0_attn_Wo))
    bq, bk, bv, bo = (np.asarray(a, f32) for a in
                      (L0_attn_bq, L0_attn_bk, L0_attn_bv, L0_attn_bo))
    P = np.asarray(L0_attn_P, f32)
    gw = np.asarray(L0_ffn_gate_w, f32)
    w1 = np.asarray(L0_ffn_w1, f32)
    w2 = np.asarray(L0_ffn_w2, f32)

    def tiles(W):
        return np.ascontiguousarray(
            W.T.reshape(8, 128, 8, 128).transpose(2, 0, 1, 3))

    shared = {
        "wq_t4": tiles(Wq), "wk_t4": tiles(Wk), "wo_t4": tiles(Wo),
        "wvT": np.ascontiguousarray(Wv.T),
        "bq_c": np.ascontiguousarray(bq.reshape(8, 128).T),
        "bk_c": np.ascontiguousarray(bk.reshape(8, 128).T),
        "bo_c": np.ascontiguousarray(bo.reshape(8, 128).T),
        "bv_r": bv.reshape(1, D),
        "pT": np.ascontiguousarray(P.T),
        "gwT": np.ascontiguousarray(gw.T),
        "w1r": np.ascontiguousarray(w1.reshape(E, 1024, 32, 128).transpose(0, 2, 1, 3)),
        "w2r": np.ascontiguousarray(w2),
    }
    shared.update(_consts())

    in_maps = []
    for c in range(8):
        b, h = c // 2, c % 2
        xb_ = x[b]
        if h:
            xb_ = np.concatenate([xb_[TOK:], xb_[:TOK]], axis=0)
        m = dict(shared)
        m["xT"] = np.ascontiguousarray(xb_.T)
        in_maps.append(m)

    nc, _ = _get_program(_debug_level, _stop)
    res = run_bass_kernel_spmd(nc, in_maps, core_ids=list(range(8)))
    out = np.empty((B, N, D), f32)
    for c in range(8):
        b, h = c // 2, c % 2
        shard = np.asarray(jax.device_get(res.results[c]["out"]))
        out[b, h * TOK:(h + 1) * TOK] = shard
    if _debug_level:
        return out, res
    return out
